# version 02
# baseline (speedup 1.0000x reference)
import contextlib
import numpy as np
import ml_dtypes
import concourse.bass as bass
import concourse.mybir as mybir
from concourse.bass_utils import run_bass_kernel_spmd

F32 = mybir.dt.float32
BF16 = mybir.dt.bfloat16
AF = mybir.ActivationFunctionType
ALU = mybir.AluOpType
AX = mybir.AxisListType

D=1024; DC=8; L=4096; BATCH=16; NCORE=8; BCn=2
T=1024; NT=4; NTT=8; SPAN=1536; HALO=256
NH=16; HD=64; W=256; NAB=4; KJ=768
NS=64; DTR=64; SE=4096; CD=128; NE=8; HF=4096
EPS=1e-5
ENABLE = {"ssm", "attn", "conv", "mem", "ffn"}
RUN_P0 = True
RUN_TILES = True
DBG_REQ = []
PAIRS = [(0,4),(1,5),(2,6),(3,7),(8,12),(9,13),(10,14),(11,15)]

WORDER = ([("xproj", DC*192), ("wdt", 1024), ("wk", DC*256), ("wv", DC*256)]
          + [("wsm%d"%oc, DC*128) for oc in range(DC)]
          + [("wq%d"%p, DC*128) for p in range(8)]
          + [("wo%d"%oc, DC*128) for oc in range(DC)]
          + [("wwm%d"%oc, DC*128) for oc in range(DC)])
WIDX = {nm: i for i, (nm, _) in enumerate(WORDER)}
WCOLS = dict(WORDER)


def _build_A3(rel_bias):
    qi = np.arange(W)[:, None]; kj = np.arange(KJ)[None, :]
    rel = kj - W - qi
    win = np.abs(rel) <= W
    bias = rel_bias[np.clip(rel + W, 0, 2 * W)]
    out = np.empty((3, NH, 2, 128, KJ), np.float32)
    for v in range(3):
        pad = (kj >= W) if v == 0 else ((kj < 2 * W) if v == 2 else np.ones_like(kj, bool))
        m = win & pad
        for h in range(NH):
            a = np.where(m, bias[:, :, h], np.float32(-1e9)).astype(np.float32)
            out[v, h, 0], out[v, h, 1] = a[:128], a[128:]
    return out


class Tok:
    __slots__ = ("eng", "seq")
    def __init__(self, eng, seq):
        self.eng = eng; self.seq = seq


class Rec:
    def __init__(self, name):
        self.name = name; self.ops = []
    def op(self, method, *args, after=(), **kw):
        t = Tok(self.name, len(self.ops))
        self.ops.append([method, args, kw, [a for a in after if a is not None], t])
        return t


class SPRec:
    def __init__(self):
        self.ops = []; self.n = 0; self.cnt = {}
    def dma(self, sem, out_ap, in_ap, after=(), slow=False):
        k = self.cnt.get(sem, 0); self.cnt[sem] = k + 1
        self.ops.append([sem, k, out_ap, in_ap, [a for a in after if a is not None], slow])
        return ("spd", sem, k + 1)


def kernel(**inputs):
    xfull = np.ascontiguousarray(inputs["L0_attn_x"])
    A3 = _build_A3(np.ascontiguousarray(inputs["L0_attn_rel_bias"]))
    rsel = np.zeros((NS, NS * 128), np.float32)
    for n in range(NS):
        rsel[n, n * 128:(n + 1) * 128] = 1.0
    rsel8 = np.zeros((NE, NE * 128), np.float32)
    for e in range(NE):
        rsel8[e, e * 128:(e + 1) * 128] = 1.0
    common = {
        "ln_g": inputs["L0_attn_ln_g"], "ln_b": inputs["L0_attn_ln_b"],
        "w_xproj": inputs["L0_attn_w_xproj"], "w_dt": inputs["L0_attn_w_dt"],
        "b_dt": inputs["L0_attn_b_dt"], "A_log": inputs["L0_attn_A_log"],
        "D_ssm": inputs["L0_attn_D_ssm"], "w_ssm_out": inputs["L0_attn_w_ssm_out"],
        "wq": inputs["L0_attn_wq"], "wk": inputs["L0_attn_wk"], "wv": inputs["L0_attn_wv"],
        "wo": inputs["L0_attn_wo"], "w_wm": inputs["L0_attn_w_wm"], "b_wm": inputs["L0_attn_b_wm"],
        "episodic": inputs["L0_attn_episodic"][0], "w_dc1": inputs["L0_attn_w_dc1"],
        "b_dc1": inputs["L0_attn_b_dc1"], "w_dc2": inputs["L0_attn_w_dc2"],
        "b_dc2": inputs["L0_attn_b_dc2"], "w_conv": inputs["L0_attn_w_conv"],
        "b_conv": inputs["L0_attn_b_conv"], "router_w": inputs["L0_ffn_router_w"],
        "up_w": inputs["L0_ffn_up_w"], "down_w": inputs["L0_ffn_down_w"],
        "A3": A3, "zeros_pad": np.zeros((512, D), np.float32),
        "rsel": rsel.astype(ml_dtypes.bfloat16), "rsel8": rsel8.astype(ml_dtypes.bfloat16),
        "ident": np.eye(128, dtype=ml_dtypes.bfloat16), "identf": np.eye(128, dtype=np.float32),
        "onesc": np.ones((128, 2), np.float32), "epsc": np.full((128, 2), 1e-5, np.float32), "onescb": np.ones((128, 2), ml_dtypes.bfloat16),
        "onesr": np.ones((2, 128), np.float32), "onesrb": np.ones((2, 128), ml_dtypes.bfloat16),
    }
    common = {k: np.ascontiguousarray(v) for k, v in common.items()}
    in_maps = []
    for c in range(NCORE):
        m = dict(common)
        m["x_in"] = np.ascontiguousarray(xfull[2 * c:2 * c + 2])
        in_maps.append(m)

    nc, outnames = _build(bool(DBG_REQ))
    res = run_bass_kernel_spmd(nc, in_maps, core_ids=list(range(NCORE)))
    outs = np.empty((BATCH, L, D), np.float32)
    for c in range(NCORE):
        outs[2 * c:2 * c + 2] = res.results[c]["out"]
    if DBG_REQ:
        kernel.dbg = [{k: r[k] for k in r if k.startswith("dbg_")} for r in res.results]
    return outs


def _build(with_dbg=False):
    nc = bass.Bass("TRN2", target_bir_lowering=False)
    x_in = nc.dram_tensor("x_in", [BCn, L, D], F32, kind="ExternalInput").ap()
    out_d = nc.dram_tensor("out", [BCn, L, D], F32, kind="ExternalOutput").ap()

    def inp(name, shape, dtp=F32):
        return nc.dram_tensor(name, list(shape), dtp, kind="ExternalInput").ap()

    IN = {}
    for name, shape in [("ln_g",[D]),("ln_b",[D]),("w_xproj",[D,192]),("w_dt",[DTR,D]),
                        ("b_dt",[D]),("A_log",[D,NS]),("D_ssm",[D]),("w_ssm_out",[D,D]),
                        ("wq",[D,D]),("wk",[D,256]),("wv",[D,256]),("wo",[D,D]),
                        ("w_wm",[D,D]),("b_wm",[D]),("episodic",[SE,CD]),("w_dc1",[CD,512]),
                        ("b_dc1",[512]),("w_dc2",[512,D]),("b_dc2",[D]),("w_conv",[D,4]),
                        ("b_conv",[D]),("router_w",[D,NE]),("up_w",[NE,D,HF]),
                        ("down_w",[NE,HF,D]),("A3",[3,NH,2,128,KJ]),("zeros_pad",[512,D]),
                        ("identf",[128,128]),("onesc",[128,2]),("onesr",[2,128]),("epsc",[128,2])]:
        IN[name] = inp(name, shape)
    for name, shape in [("rsel",[NS,NS*128]),("rsel8",[NE,NE*128]),("ident",[128,128]),
                        ("onescb",[128,2]),("onesrb",[2,128])]:
        IN[name] = inp(name, shape, BF16)

    def scr(name, shape, dtp=BF16):
        return nc.dram_tensor(name, list(shape), dtp, kind="Internal").ap()

    SC = {"wq": scr("sc_wq",[DC,128,D]), "wk": scr("sc_wk",[DC,128,256]),
          "wv": scr("sc_wv",[DC,128,256]), "wo": scr("sc_wo",[DC,128,D]),
          "wsm": scr("sc_wsm",[DC,128,D]), "wwm": scr("sc_wwm",[DC,128,D]),
          "xproj": scr("sc_xproj",[DC,128,192]), "wdt": scr("sc_wdt",[DTR,D]),
          "up": scr("sc_up",[NE,DC,128,HF]), "down": scr("sc_down",[NE,HF//128,128,D]),
          "decf": scr("sc_decf",[DC,128,SE]), "decs": scr("sc_decs",[SE//128,128,D]),
          "A": scr("sc_A",[3,NH,2,128,KJ])}

    st = contextlib.ExitStack()
    def alloc(name, cols):
        return st.enter_context(nc.sbuf_tensor(name, [128, cols], F32))
    R_acc = alloc("R_acc", 8192); R_x = alloc("R_x", 6144); R_xn = alloc("R_xn", 5120)
    R_y = alloc("R_y", 8192); R_uao = alloc("R_uao", 4096); R_del = alloc("R_del", 8192)
    R_k = alloc("R_k", 1536); R_v = alloc("R_v", 1536); R_w = alloc("R_w", 2048)
    R_u3 = alloc("R_u3", 5760); R_pp = alloc("R_pp", 2176)
    bfc = lambda ap: ap.bitcast(BF16)

    pp = R_pp[:]
    carry = pp[:, 0:512]
    A_fm = pp[:, 512:1024]
    ident_b = bfc(pp[:, 1024:1088])
    ident_f = pp[:, 1088:1216]
    onesc = pp[:, 1216:1218]
    onescb = bfc(pp[:, 1218:1219])
    COLS = {}
    _c = 1220
    for nm in ["ln_g","ln_b","b_dt","D_ssm","b_conv","b_wm2","b_dc2"]:
        COLS[nm] = pp[:, _c:_c+DC]; _c += DC
    wconv_c = pp[:, _c:_c+4*DC]; _c += 4*DC
    rsel_b = bfc(pp[:, _c:_c+128]); _c += 128
    rsel8_b = bfc(pp[:, _c:_c+128]); _c += 128
    routw_b = bfc(pp[:, _c:_c+4*DC]); _c += 4*DC
    bdc1_c = pp[:, _c:_c+4]; _c += 4
    onesr_f = pp[:, _c:_c+128]; _c += 128
    onesr_b = bfc(pp[:, _c:_c+64]); _c += 64
    epsc = pp[:, _c:_c+2]; _c += 2
    assert _c <= 2176, _c

    acc = R_acc[:]
    xsp = bfc(R_x[:])
    xn = bfc(R_xn[:, 0:4096]); yln = xn
    decf_s = bfc(R_xn[:, 0:4096])
    decs_s = bfc(R_xn[:, 4096:5120])
    y = R_y[:]
    h_fm = bfc(R_y[:])
    stage = R_y[:, 0:4096]
    u = bfc(R_uao[:]); ao = bfc(R_uao[:])
    rowtmp = R_uao[0:1, 0:1536]
    delta = R_del[:]
    es_sb = R_del[:, 0:4096]
    p_es = bfc(R_del[:, 4096:6144])
    pT_es = bfc(R_del[:, 6144:8192])
    ffn_part = R_del[:]
    k_fm = bfc(R_k[:])
    v_tm = bfc(R_v[:])
    wsl = bfc(R_w[:])
    u3 = R_u3[:]
    decay = bfc(u3[:, 0:1024]); uBv = bfc(u3[:, 1024:2048]); hhv = bfc(u3[:, 2048:3072])
    Bbc = bfc(u3[:, 3072:3584]); Cbc = bfc(u3[:, 3584:4096])
    BCr = bfc(u3[:, 4096:5120])
    dtin = bfc(R_xn[:, 4096:5120])
    halo_f = bfc(u3[:, 0:2048])
    mu_b = u3[:, 2048:3584]; r_b = u3[:, 3584:5120]; sq_s = u3[:, 5120:5632]
    mx_c = u3[:, 5632:5640]
    q_jit = bfc(u3[:, 0:1024])
    A_jit = bfc(u3[:, 1024:2560])
    s_at = u3[:, 2560:3328]
    p_at = bfc(u3[:, 3328:3712])
    pT_at = bfc(u3[:, 3712:4160])
    ostage = u3[:, 0:2048]
    g_sb = bfc(u3[:, 2048:2560])
    g_bc = bfc(u3[:, 2560:3072])
    rt_row = u3[:, 3072:4096]
    rt_tm = u3[:, 4160:4224]
    gt_tm = bfc(u3[:, 4224:4256])
    rt_sc = u3[:, 4256:4288]

    ps_all = st.enter_context(nc.psum_tensor("ps_all", [128, 8, 512], F32))[:]
    psT = bfc(ps_all[:, 4, :])

    semnames = ["mStage","mW","mA","mRsel","mDecF","mDecS","mFfn","mG8","mOut","mP0",
                "cPE","cDV","cAC","cPO"]
    SEM = {n: st.enter_context(nc.semaphore(n)) for n in semnames}

    dbg = {}
    if with_dbg:
        for nm in DBG_REQ:
            dbg[nm] = nc.dram_tensor("dbg_" + nm, [NTT, 128, 8192], F32, kind="ExternalOutput").ap()

    pe, dv, ac, po, sp = Rec("cPE"), Rec("cDV"), Rec("cAC"), Rec("cPO"), SPRec()
    M = {"cPE": pe, "cDV": dv, "cAC": ac, "cPO": po}
    psl = {i: None for i in range(8)}

    def XS(dc, a, b): return xsp[:, dc*SPAN+a: dc*SPAN+b]
    def XCr(dc, a, b): return XS(dc, HALO+a, HALO+b)
    def ACCs(dc, a, b): return acc[:, dc*T+a: dc*T+b]
    def XNs(dc, a, b): return xn[:, dc*T+a: dc*T+b]
    def YSs(dc, a, b): return y[:, dc*T+a: dc*T+b]
    def DLs(dc, a, b): return delta[:, dc*T+a: dc*T+b]
    def USs(dc, a, b): return u[:, dc*T+a: dc*T+b]
    def AOs(dc, a, b): return ao[:, dc*T+a: dc*T+b]

    def wslot(nm):
        return wsl[:, (WIDX[nm] % 2)*2048:(WIDX[nm] % 2)*2048+2048]

    def wlh(nm, dc, c0, c1):
        ncol = WCOLS[nm] // DC
        return wslot(nm)[:, dc*ncol+c0: dc*ncol+c1]

    def wwait(nm):
        n = 0
        for onm, _ in WORDER:
            n += 2 if onm.startswith("wq") else 1
            if onm == nm:
                return ("spd", "mW", n)
        raise KeyError(nm)

    wfree = {}

    stage_ld = []
    for c in range(6):
        def mk(c):
            def f(s, t):
                lo = t*T - HALO + c*256
                if lo < 0 or lo >= L:
                    return IN["zeros_pad"][0:256, :].rearrange("(tb p) d -> p tb d", p=128)
                return x_in[s, lo:lo+256, :].rearrange("(tb p) d -> p tb d", p=128)
            return f
        dst = stage[:, (c % 2)*2048:(c % 2)*2048+2048].rearrange("p (tb d) -> p tb d", tb=2)
        stage_ld.append(sp.dma("mStage", dst, mk(c), after=(("prevtile", None),)))

    tr_round = []
    rd = 0
    for c in range(6):
        for r in range(4):
            bk = rd % 4
            last = None
            for j in range(4):
                k = r*4 + j
                tb, dc = divmod(k, 8)
                src = stage[:, (c % 2)*2048 + tb*1024 + dc*128: (c % 2)*2048 + tb*1024 + dc*128 + 128]
                aft = []
                if j == 0:
                    aft.append(("spd", "mStage", c+1))
                    if psl[bk] is not None:
                        aft.append(psl[bk])
                last = pe.op("transpose", out=ps_all[:, bk, j*128:(j+1)*128], in_=src,
                             identity=ident_f, after=tuple(aft))
            tr_round.append(last)
            dvt = None
            for j in range(4):
                k = r*4 + j
                tb, dc = divmod(k, 8)
                g = c*256 + tb*128
                if g < HALO:
                    dst = halo_f[:, dc*512 + g: dc*512 + g + 128]
                elif g >= HALO + T:
                    dst = halo_f[:, dc*512 + 256 + (g-HALO-T): dc*512 + 256 + (g-HALO-T) + 128]
                else:
                    dst = ACCs(dc, g-HALO, g-HALO+128)
                dvt = dv.op("tensor_copy", out=dst, in_=ps_all[:, bk, j*128:(j+1)*128],
                            after=(last,) if j == 0 else ())
            psl[bk] = dvt
            rd += 1
    for c in range(6):
        tokk = tr_round[((c-2) % 6)*4 + 3]
        sp.ops[c][4].append(("prev" if c < 2 else "cur", tokk))
    tr_done_dv = psl[3]

    def oln_piece(kind, dc, qc):
        return ACCs(dc, qc*512, qc*512+512) if kind == "c" else halo_f[:, dc*512: dc*512+512]

    grp_defs = [("c0", "c", 0), ("c1", "c", 1), ("hh", "h", 0)]
    sum_toks = []
    for gi, (gnm, kind, qc) in enumerate(grp_defs):
        t = None
        for i in range(8):
            aft = []
            if i == 0:
                if psl[4] is not None:
                    aft.append(psl[4])
                aft.append(tr_done_dv)
            t = pe.op("matmul", out=ps_all[0:1, 4, :],
                      lhsT=(onesc[:, 0:1] if kind == "c" else onescb[:, 0:1]),
                      rhs=oln_piece(kind, i, qc), start=(i == 0), stop=(i == 7), after=tuple(aft))
        if gnm == "hh":
            t2 = dv.op("tensor_copy", out=mu_b[0:1, 0:256], in_=ps_all[0:1, 4, 0:256], after=(t,))
            t2 = dv.op("tensor_copy", out=mu_b[0:1, 1280:1536], in_=ps_all[0:1, 4, 256:512], after=(t,))
        else:
            t2 = dv.op("tensor_copy", out=mu_b[0:1, HALO+qc*512:HALO+qc*512+512],
                       in_=ps_all[0:1, 4, :], after=(t,))
        psl[4] = t2
        sum_toks.append(t2)
    sq_last = None
    for gi, (gnm, kind, qc) in enumerate(grp_defs):
        t = None
        for i in range(8):
            aft = [tr_done_dv] if (gi == 0 and i == 0) else []
            if sq_last is not None:
                aft.append(sq_last)
            tq = ac.op("activation", out=sq_s[:], in_=oln_piece(kind, i, qc), func=AF.Square,
                       after=tuple(aft))
            aft2 = [tq]
            if i == 0 and psl[5] is not None:
                aft2.append(psl[5])
            t = pe.op("matmul", out=ps_all[0:1, 5, :], lhsT=onesc[:, 0:1], rhs=sq_s[:],
                      start=(i == 0), stop=(i == 7), after=tuple(aft2))
            sq_last = t
        if gnm == "hh":
            t2 = dv.op("tensor_copy", out=r_b[0:1, 0:256], in_=ps_all[0:1, 5, 0:256], after=(t,))
            t2 = dv.op("tensor_copy", out=r_b[0:1, 1280:1536], in_=ps_all[0:1, 5, 256:512], after=(t,))
        else:
            t2 = dv.op("tensor_copy", out=r_b[0:1, HALO+qc*512:HALO+qc*512+512],
                       in_=ps_all[0:1, 5, :], after=(t,))
        psl[5] = t2
        sum_toks.append(t2)

    def ln_rows(dep):
        t1 = dv.op("tensor_scalar", out=mu_b[0:1, 0:SPAN], in0=mu_b[0:1, 0:SPAN],
                   scalar1=1.0/D, scalar2=None, op0=ALU.mult, after=dep)
        t2 = dv.op("tensor_scalar", out=r_b[0:1, 0:SPAN], in0=r_b[0:1, 0:SPAN],
                   scalar1=1.0/D, scalar2=None, op0=ALU.mult, after=dep)
        t3 = dv.op("tensor_tensor", out=rowtmp[0:1, 0:SPAN], in0=mu_b[0:1, 0:SPAN],
                   in1=mu_b[0:1, 0:SPAN], op=ALU.mult, after=(t1,))
        t4 = dv.op("tensor_tensor", out=r_b[0:1, 0:SPAN], in0=r_b[0:1, 0:SPAN],
                   in1=rowtmp[0:1, 0:SPAN], op=ALU.subtract, after=(t2, t3))
        t5 = ac.op("activation", out=r_b[0:1, 0:SPAN], in_=r_b[0:1, 0:SPAN], func=AF.Ln,
                   bias=epsc[0:1, 0:1], after=(t4,))
        t6 = ac.op("activation", out=r_b[0:1, 0:SPAN], in_=r_b[0:1, 0:SPAN], func=AF.Exp,
                   scale=-0.5, after=(t5,))
        return t6

    rows_done = ln_rows(tuple(sum_toks))
    bc_toks = []
    for src, dstf in ((mu_b, mu_b), (r_b, r_b)):
        for ch in range(3):
            aft = [rows_done]
            if psl[6] is not None:
                aft.append(psl[6])
            t = pe.op("matmul", out=ps_all[:, 6, :], lhsT=onesr_f[0:1, :], rhs=src[0:1, ch*512:(ch+1)*512],
                      start=True, stop=True, after=tuple(aft))
            t2 = dv.op("tensor_copy", out=dstf[:, ch*512:(ch+1)*512], in_=ps_all[:, 6, :], after=(t,))
            psl[6] = t2
            bc_toks.append(t2)
    obc = tuple(bc_toks)
    x_toks = []
    for dc in range(DC):
        for (a0, a1, src, m0) in ((0, 256, halo_f[:, dc*512:dc*512+256], 0),
                                  (1280, 1536, halo_f[:, dc*512+256:dc*512+512], 1280),
                                  (256, 768, ACCs(dc, 0, 512), 256),
                                  (768, 1280, ACCs(dc, 512, 1024), 768)):
            tt = dv.op("tensor_tensor", out=sq_s[:, 0:a1-a0], in0=src, in1=mu_b[:, a0:a1],
                       op=ALU.subtract, after=obc if dc == 0 and a0 == 0 else ())
            tx = dv.op("tensor_tensor", out=XS(dc, a0, a1), in0=sq_s[:, 0:a1-a0],
                       in1=r_b[:, a0:a1], op=ALU.mult, after=(tt,))
            x_toks.append(tx)
    x_done = tuple(x_toks[-4:])
    x_all = tuple(x_toks)

    accx_toks = []
    for dc in range(DC):
        tt = dv.op("tensor_tensor", out=ACCs(dc, 0, T), in0=ACCs(dc, 0, T),
                   in1=XCr(dc, 0, T), op=ALU.add, after=x_all)
        accx_toks.append(tt)
    accx_done = tuple(accx_toks)

    iln_sums = []
    for qc in range(2):
        t = None
        for dc in range(DC):
            aft = [x_done[-1]] if dc == 0 else []
            if dc == 0 and psl[4] is not None:
                aft.append(psl[4])
            t = pe.op("matmul", out=ps_all[0:1, 4, :], lhsT=onescb[:, 0:1],
                      rhs=XCr(dc, qc*512, qc*512+512), start=(dc == 0), stop=(dc == 7),
                      after=tuple(aft))
        t2 = dv.op("tensor_copy", out=mu_b[0:1, qc*512:qc*512+512], in_=ps_all[0:1, 4, :], after=(t,))
        psl[4] = t2
        iln_sums.append(t2)
    sqb = bfc(sq_s)
    sq_last2 = None
    for qc in range(2):
        t = None
        for dc in range(DC):
            aft = [x_done[-1]] if (qc == 0 and dc == 0) else []
            if sq_last2 is not None:
                aft.append(sq_last2)
            tq = ac.op("activation", out=sqb[:, 0:512], in_=XCr(dc, qc*512, qc*512+512),
                       func=AF.Square, after=tuple(aft))
            aft2 = [tq]
            if dc == 0 and psl[5] is not None:
                aft2.append(psl[5])
            t = pe.op("matmul", out=ps_all[0:1, 5, :], lhsT=onescb[:, 0:1], rhs=sqb[:, 0:512],
                      start=(dc == 0), stop=(dc == 7), after=tuple(aft2))
            sq_last2 = t
        t2 = dv.op("tensor_copy", out=r_b[0:1, qc*512:qc*512+512], in_=ps_all[0:1, 5, :], after=(t,))
        psl[5] = t2
        iln_sums.append(t2)

    def ln_rows2(dep, n):
        t1 = dv.op("tensor_scalar", out=mu_b[0:1, 0:n], in0=mu_b[0:1, 0:n],
                   scalar1=1.0/D, scalar2=None, op0=ALU.mult, after=dep)
        t2 = dv.op("tensor_scalar", out=r_b[0:1, 0:n], in0=r_b[0:1, 0:n],
                   scalar1=1.0/D, scalar2=None, op0=ALU.mult, after=dep)
        t3 = dv.op("tensor_tensor", out=rowtmp[0:1, 0:n], in0=mu_b[0:1, 0:n],
                   in1=mu_b[0:1, 0:n], op=ALU.mult, after=(t1,))
        t4 = dv.op("tensor_tensor", out=r_b[0:1, 0:n], in0=r_b[0:1, 0:n],
                   in1=rowtmp[0:1, 0:n], op=ALU.subtract, after=(t2, t3))
        t5 = ac.op("activation", out=r_b[0:1, 0:n], in_=r_b[0:1, 0:n], func=AF.Ln,
                   bias=epsc[0:1, 0:1], after=(t4,))
        t6 = ac.op("activation", out=r_b[0:1, 0:n], in_=r_b[0:1, 0:n], func=AF.Exp,
                   scale=-0.5, after=(t5,))
        return t6

    irows = ln_rows2(tuple(iln_sums), T)
    ibc = []
    for src in (mu_b, r_b):
        for ch in range(2):
            aft = [irows]
            if psl[6] is not None:
                aft.append(psl[6])
            t = pe.op("matmul", out=ps_all[:, 6, :], lhsT=onesr_f[0:1, :],
                      rhs=src[0:1, ch*512:(ch+1)*512], start=True, stop=True, after=tuple(aft))
            t2 = dv.op("tensor_copy", out=src[:, ch*512:(ch+1)*512], in_=ps_all[:, 6, :], after=(t,))
            psl[6] = t2
            ibc.append(t2)
    ibc = tuple(ibc)
    xn_toks = []
    for dc in range(DC):
        for qc in range(2):
            a0, a1 = qc*512, qc*512+512
            tt = dv.op("tensor_tensor", out=sq_s[:, 0:512], in0=XCr(dc, a0, a1),
                       in1=mu_b[:, a0:a1], op=ALU.subtract, after=ibc if dc == 0 and qc == 0 else ())
            tt = dv.op("tensor_tensor", out=sq_s[:, 0:512], in0=sq_s[:, 0:512],
                       in1=r_b[:, a0:a1], op=ALU.mult, after=(tt,))
            tt = dv.op("tensor_scalar", out=XNs(dc, a0, a1), in0=sq_s[:, 0:512],
                       scalar1=COLS["ln_g"][:, dc:dc+1], scalar2=COLS["ln_b"][:, dc:dc+1],
                       op0=ALU.mult, op1=ALU.add, after=(tt,))
            xn_toks.append(tt)
    xn_done = tuple(xn_toks)

    wld = {}
    for nm, cols in WORDER:
        i = WIDX[nm]
        dst = wslot(nm)[:, 0:cols]
        aft = [("prev" if i < 2 else "cur", None, nm, i)]
        if nm == "xproj":
            hnd = sp.dma("mW", dst.rearrange("p (dc c) -> p dc c", dc=DC),
                         SC["xproj"].rearrange("dc p c -> p dc c"))
        elif nm == "wdt":
            hnd = sp.dma("mW", wslot(nm)[0:DTR, 0:1024], SC["wdt"][:, :])
        elif nm == "wk":
            hnd = sp.dma("mW", dst.rearrange("p (dc c) -> p dc c", dc=DC),
                         SC["wk"].rearrange("dc p c -> p dc c"))
        elif nm == "wv":
            hnd = sp.dma("mW", dst.rearrange("p (dc c) -> p dc c", dc=DC),
                         SC["wv"].rearrange("dc p c -> p dc c"))
        elif nm.startswith("wq"):
            p = int(nm[2:]); h0, h1 = PAIRS[p]
            d2 = dst.rearrange("p (dc c) -> p dc c", dc=DC)
            hnd = sp.dma("mW", d2[:, :, 0:64], SC["wq"][:, :, h0*64:(h0+1)*64])
            hnd = sp.dma("mW", d2[:, :, 64:128], SC["wq"][:, :, h1*64:(h1+1)*64])
        else:
            base = {"ws": "wsm", "wo": "wo", "ww": "wwm"}[nm[:2]]
            oc = int(nm[3:]) if base == "wwm" else int(nm[2:]) if base == "wo" else int(nm[3:])
            hnd = sp.dma("mW", dst.rearrange("p (dc c) -> p dc c", dc=DC),
                         SC[base][:, :, oc*128:(oc+1)*128].rearrange("dc p c -> p dc c"))
        wld[nm] = hnd

    xp_evac = []
    for oc in range(2):
        for qc in range(2):
            t = None
            for dc in range(DC):
                aft = list(xn_done) if (oc == 0 and qc == 0 and dc == 0) else []
                aft.append(wwait("xproj"))
                bk = 7
                if dc == 0 and psl[bk] is not None:
                    aft.append(psl[bk])
                lhs = wlh("xproj", dc, 0, 128) if oc == 0 else wlh("xproj", dc, 128, 192)
                t = pe.op("matmul", out=ps_all[0:128 if oc == 0 else 64, bk, :],
                          lhsT=lhs, rhs=XNs(dc, qc*512, qc*512+512),
                          start=(dc == 0), stop=(dc == 7), after=tuple(aft))
            if oc == 0:
                e1 = ac.op("activation", out=dtin[0:64, qc*512:qc*512+512],
                           in_=ps_all[0:64, 7, :], func=AF.Identity, after=(t,))
                e2 = ac.op("activation", out=BCr[64:128, qc*512:qc*512+512],
                           in_=ps_all[64:128, 7, :], func=AF.Identity, after=(t,))
                psl[7] = e2
                xp_evac += [e1, e2]
            else:
                e1 = ac.op("activation", out=BCr[0:64, qc*512:qc*512+512],
                           in_=ps_all[0:64, 7, :], func=AF.Identity, after=(t,))
                psl[7] = e1
                xp_evac.append(e1)
    wfree["xproj"] = t
    xp_done = tuple(xp_evac)

    del_toks = []
    prev_t5 = None
    for oc in range(DC):
        for qc in range(2):
            aft = [wwait("wdt")] + [tok for tok in xp_done]
            bk = oc % 2
            if psl[bk] is not None:
                aft.append(psl[bk])
            t = pe.op("matmul", out=ps_all[:, bk, :], lhsT=wslot("wdt")[0:DTR, oc*128:(oc+1)*128],
                      rhs=dtin[0:64, qc*512:qc*512+512], start=True, stop=True, after=tuple(aft))
            b_c = COLS["b_dt"][:, oc:oc+1]
            a1 = ac.op("activation", out=sq_s[:, 0:512], in_=ps_all[:, bk, :], func=AF.Abs,
                       bias=b_c, after=(t,) if prev_t5 is None else (t, prev_t5))
            a2 = ac.op("activation", out=sq_s[:, 0:512], in_=sq_s[:, 0:512], func=AF.Exp,
                       scale=-1.0, after=(a1,))
            a3 = ac.op("activation", out=sq_s[:, 0:512], in_=sq_s[:, 0:512], func=AF.Ln,
                       bias=onesc[:, 0:1], after=(a2,))
            a4 = ac.op("activation", out=DLs(oc, qc*512, qc*512+512), in_=ps_all[:, bk, :],
                       func=AF.Relu, bias=b_c, after=(a3,))
            t5 = dv.op("tensor_tensor", out=DLs(oc, qc*512, qc*512+512),
                       in0=DLs(oc, qc*512, qc*512+512), in1=sq_s[:, 0:512], op=ALU.add,
                       after=(a4,))
            psl[bk] = t5
            prev_t5 = t5
            del_toks.append(t5)
    wfree["wdt"] = t
    del_done = tuple(del_toks)

    u_toks = []
    for dc in range(DC):
        tt = dv.op("tensor_tensor", out=USs(dc, 0, T), in0=DLs(dc, 0, T), in1=XNs(dc, 0, T),
                   op=ALU.mult, after=del_done if dc == 0 else ())
        u_toks.append(tt)
    u_done = tuple(u_toks)

    kv_toks = []
    k_done = {}
    if "attn" in ENABLE:
        for kc in range(2):
            for qc in range(3):
                t = None
                for dc in range(DC):
                    aft = [wwait("wk")] + ([x_all[-1]] if (kc == 0 and qc == 0 and dc == 0) else [])
                    bk = 0 if kc == 0 else 1
                    if dc == 0 and psl[bk] is not None:
                        aft.append(psl[bk])
                    t = pe.op("matmul", out=ps_all[:, bk, :],
                              lhsT=wlh("wk", dc, kc*128, kc*128+128),
                              rhs=XS(dc, qc*512, qc*512+512), start=(dc == 0), stop=(dc == 7),
                              after=tuple(aft))
                e = ac.op("activation", out=k_fm[:, kc*1536+qc*512: kc*1536+qc*512+512],
                          in_=ps_all[:, 0 if kc == 0 else 1, :], func=AF.Identity, after=(t,))
                psl[0 if kc == 0 else 1] = e
                kv_toks.append(e)
        wfree["wk"] = t
        for tb in range(12):
            t = None
            bk = 2 + (tb % 2)
            for dc in range(DC):
                aft = [wwait("wv")]
                if dc == 0 and psl[bk] is not None:
                    aft.append(psl[bk])
                t = pe.op("matmul", out=ps_all[:, bk, 0:256], lhsT=XS(dc, tb*128, tb*128+128),
                          rhs=wlh("wv", dc, 0, 256), start=(dc == 0), stop=(dc == 7),
                          after=tuple(aft))
            e = ac.op("activation", out=v_tm[:, tb*256:(tb+1)*256], in_=ps_all[:, bk, 0:256],
                      func=AF.Identity, after=(t,))
            psl[bk] = e
            kv_toks.append(e)
        wfree["wv"] = t
    kv_done = tuple(kv_toks)

    y_done = ()
    if "ssm" in ENABLE:
        rs_toks = []
        bc_free = [None, None]
        dec_free = [None, None]
        ub_free = [None, None]
        hh_free = [None, None]
        prev_scan = {}
        ylast = {dc: None for dc in range(DC)}
        for n in range(NS):
            sl = n % 2
            rb = rsel_b.rearrange("p (s c) -> p s c", s=2)
            h1 = sp.dma("mRsel", rb[0:64, sl, :], IN["rsel"].rearrange("n (k c) -> n k c", k=NS)[n, 0:64, :],
                        after=((("cur", rs_toks[2*(n-2)+1]) if n >= 2 else ("prevRsel", sl)),))
            h2 = sp.dma("mRsel", rb[64:128, sl, :], IN["rsel"].rearrange("n (k c) -> n k c", k=NS)[n, 0:64, :],
                        after=())
            tB = None; tC = None
            for half in range(2):
                aft = [("spd", "mRsel", 2*n+2), xp_done[-1]]
                if psl[6] is not None:
                    aft.append(psl[6])
                if bc_free[0] is not None and half == 0:
                    aft.append(bc_free[0])
                tB = pe.op("matmul", out=ps_all[:, 6, :], lhsT=rb[64:128, sl, :],
                           rhs=BCr[64:128, half*512:(half+1)*512], start=True, stop=True,
                           after=tuple(aft))
                eB = ac.op("activation", out=Bbc[:, half*512:(half+1)*512], in_=ps_all[:, 6, :],
                           func=AF.Identity, after=(tB,))
                psl[6] = eB
                aft = []
                if psl[7] is not None:
                    aft.append(psl[7])
                if bc_free[1] is not None and half == 0:
                    aft.append(bc_free[1])
                tC = pe.op("matmul", out=ps_all[:, 7, :], lhsT=rb[0:64, sl, :],
                           rhs=BCr[0:64, half*512:(half+1)*512], start=True, stop=True,
                           after=tuple(aft))
                eC = ac.op("activation", out=Cbc[:, half*512:(half+1)*512], in_=ps_all[:, 7, :],
                           func=AF.Identity, after=(tC,))
                psl[7] = eC
            rs_toks += [tB, tC]
            bref = eB; cref = eC
            for dc in range(DC):
                k = n * DC + dc
                sl2 = k % 2
                aft = [del_done[-1]] if k == 0 else []
                if dec_free[sl2] is not None:
                    aft.append(dec_free[sl2])
                td = ac.op("activation", out=decay[:, sl2*512:(sl2+1)*512].bitcast(BF16) if False else decay.rearrange("p (s c) -> p s c", s=2)[:, sl2, :],
                           in_=DLs(dc, 0, T), func=AF.Exp,
                           scale=A_fm[:, dc*NS+n: dc*NS+n+1], after=tuple(aft))
                aft = [bref, u_done[-1] if k == 0 else None]
                if ub_free[sl2] is not None:
                    aft.append(ub_free[sl2])
                tu = po.op("tensor_tensor", out=uBv.rearrange("p (s c) -> p s c", s=2)[:, sl2, :],
                           in0=USs(dc, 0, T), in1=Bbc[:], op=ALU.mult, after=tuple(aft))
                aft = [td, tu]
                if hh_free[sl2] is not None:
                    aft.append(hh_free[sl2])
                tsc = dv.op("tensor_tensor_scan",
                            out=hhv.rearrange("p (s c) -> p s c", s=2)[:, sl2, :],
                            data0=decay.rearrange("p (s c) -> p s c", s=2)[:, sl2, :],
                            data1=uBv.rearrange("p (s c) -> p s c", s=2)[:, sl2, :],
                            initial=carry[:, dc*NS+n: dc*NS+n+1], op0=ALU.mult, op1=ALU.add,
                            after=tuple(aft))
                ub_free[sl2] = tsc
                tcar = ac.op("activation", out=carry[:, dc*NS+n: dc*NS+n+1],
                             in_=hhv.rearrange("p (s c) -> p s c", s=2)[:, sl2, T-1:T],
                             func=AF.Identity, after=(tsc,))
                thc = dv.op("tensor_tensor", out=decay.rearrange("p (s c) -> p s c", s=2)[:, sl2, :],
                            in0=hhv.rearrange("p (s c) -> p s c", s=2)[:, sl2, :],
                            in1=Cbc[:], op=ALU.mult, after=(cref, tcar))
                if n == 0:
                    ty = dv.op("tensor_copy", out=YSs(dc, 0, T),
                               in_=decay.rearrange("p (s c) -> p s c", s=2)[:, sl2, :], after=(thc,))
                else:
                    ty = dv.op("tensor_tensor", out=YSs(dc, 0, T), in0=YSs(dc, 0, T),
                               in1=decay.rearrange("p (s c) -> p s c", s=2)[:, sl2, :],
                               op=ALU.add, after=(thc,))
                dec_free[sl2] = ty
                hh_free[sl2] = thc
                ylast[dc] = ty
            bc_free = [tu, thc]
        y_done = tuple(ylast.values())

        si_toks = []
        for dc in range(DC):
            tt = dv.op("scalar_tensor_tensor", out=XNs(dc, 0, T), in0=XCr(dc, 0, T),
                       scalar=COLS["D_ssm"][:, dc:dc+1], in1=YSs(dc, 0, T),
                       op0=ALU.mult, op1=ALU.add, after=y_done)
            si_toks.append(tt)
        si_done = tuple(si_toks)
        for oc in range(DC):
            nm = "wsm%d" % oc
            for qc in range(2):
                t = None
                bk = oc % 2
                for dc in range(DC):
                    aft = [wwait(nm)] + (list(si_done) if (oc == 0 and qc == 0 and dc == 0) else [])
                    aft += list(accx_done) if (oc == 0 and qc == 0 and dc == 0) else []
                    if dc == 0 and psl[bk] is not None:
                        aft.append(psl[bk])
                    t = pe.op("matmul", out=ps_all[:, bk, :], lhsT=wlh(nm, dc, 0, 128),
                              rhs=XNs(dc, qc*512, qc*512+512), start=(dc == 0), stop=(dc == 7),
                              after=tuple(aft))
                e = dv.op("tensor_tensor", out=ACCs(oc, qc*512, qc*512+512),
                          in0=ACCs(oc, qc*512, qc*512+512), in1=ps_all[:, bk, :],
                          op=ALU.add, after=(t,))
                psl[bk] = e
                si_toks.append(e)
            wfree[nm] = t
        ssm_done = tuple(si_toks[-2:])
    else:
        ssm_done = accx_done

    col_sc = u3[:, 4256:4288]
    at_toks = []
    if "attn" in ENABLE:
        A_free = [None, None]
        sat_free = None
        pat_free = None
        pT_free = None
        qj_free = None
        it = 0
        qjv = q_jit
        for p in range(8):
            h0, h1 = PAIRS[p]
            nm = "wq%d" % p
            qtoks = []
            tlast = None
            for th in range(2):
                bk = 2 + th
                t = None
                for dc in range(DC):
                    aft = [wwait(nm)]
                    if th == 0 and dc == 0:
                        if qj_free is not None:
                            aft.append(qj_free)
                        if p == 0 and kv_done:
                            aft.append(kv_done[-1])
                        if p == 0:
                            aft += list(ssm_done)
                    if dc == 0 and psl[bk] is not None:
                        aft.append(psl[bk])
                    t = pe.op("matmul", out=ps_all[:, bk, :], lhsT=wlh(nm, dc, 0, 128),
                              rhs=XCr(dc, th*512, th*512+512), start=(dc == 0), stop=(dc == 7),
                              after=tuple(aft))
                e = ac.op("activation", out=qjv[:, th*512:th*512+512], in_=ps_all[:, bk, :],
                          func=AF.Identity, after=(t,))
                psl[bk] = e
                qtoks.append(e)
                tlast = t
            wfree[nm] = tlast
            qdone = tuple(qtoks)
            for ab in range(NAB):
                for hi, h in enumerate((h0, h1)):
                    base = hi * 64
                    kv = h // 4
                    kc = kv // 2
                    asl = it % 2
                    Ajit4 = A_jit.rearrange("p (s q c) -> p s q c", s=2, q=2)
                    hA = sp.dma("mA", Ajit4[:, asl, :, :], ("Aload", h, ab),
                                after=((("cur", A_free[asl]) if A_free[asl] is not None
                                        else ("prevA", asl)),))
                    for qh in range(2):
                        aft0 = [("spd", "mA", it + 1)]
                        if qh == 0 and hi == 0 and ab == 0:
                            aft0 += list(qdone)
                        lhs = qjv[base:base+64, ab*256+qh*128: ab*256+qh*128+128]
                        a6 = list(aft0)
                        if psl[6] is not None:
                            a6.append(psl[6])
                        t1 = pe.op("matmul", out=ps_all[:, 6, :], lhsT=lhs,
                                   rhs=k_fm[base:base+64, kc*1536+ab*256: kc*1536+ab*256+512],
                                   start=True, stop=True, after=tuple(a6))
                        a7 = []
                        if psl[7] is not None:
                            a7.append(psl[7])
                        t2 = pe.op("matmul", out=ps_all[:, 7, 0:256], lhsT=lhs,
                                   rhs=k_fm[base:base+64, kc*1536+ab*256+512: kc*1536+ab*256+768],
                                   start=True, stop=True, after=tuple(a7))
                        aft = [t1, t2]
                        if sat_free is not None:
                            aft.append(sat_free)
                        tb = dv.op("tensor_tensor", out=s_at[:],
                                   in0=ps_all[:, 6:8, :].rearrange("p a b -> p (a b)")[:, 0:768],
                                   in1=Ajit4[:, asl, qh, :], op=ALU.add, after=tuple(aft))
                        psl[6] = tb; psl[7] = tb
                        A_free[asl] = tb
                        tmx = dv.op("tensor_reduce", out=col_sc[:, 0:1], in_=s_at[:],
                                    axis=AX.X, op=ALU.max, negate=True, after=(tb,))
                        aft = [tmx]
                        if pat_free is not None:
                            aft.append(pat_free)
                        tex = ac.op("activation", out=p_at[:], in_=s_at[:], func=AF.Exp,
                                    bias=col_sc[:, 0:1], accum_out=col_sc[:, 1:2],
                                    after=tuple(aft))
                        sat_free = tex
                        trc = dv.op("reciprocal", out=col_sc[:, 2:3], in_=col_sc[:, 1:2],
                                    after=(tex,))
                        tps = dv.op("tensor_scalar", out=p_at[:], in0=p_at[:],
                                    scalar1=col_sc[:, 2:3], scalar2=None, op0=ALU.mult,
                                    after=(trc,))
                        ttr = None
                        for kjc in range(6):
                            aft = [tps]
                            if kjc == 0 and psl[4] is not None:
                                aft.append(psl[4])
                            ttr = pe.op("transpose", out=psT[:, kjc*128:(kjc+1)*128],
                                        in_=p_at[:, kjc*128:(kjc+1)*128], identity=ident_b,
                                        after=tuple(aft))
                        pat_free = ttr
                        aft = [ttr]
                        if pT_free is not None:
                            aft.append(pT_free)
                        tev = ac.op("activation", out=pT_at[:, 0:768], in_=psT[:, 0:768],
                                    func=AF.Identity, after=tuple(aft))
                        psl[4] = tev
                        b2 = (h % 2) * 64
                        t = None
                        for kjc in range(6):
                            aft = [tev]
                            if kjc == 0 and psl[5] is not None:
                                aft.append(psl[5])
                            t = pe.op("matmul", out=ps_all[b2:b2+64, 5, 0:128],
                                      lhsT=v_tm[:, (2*ab+kjc)*256 + kv*64: (2*ab+kjc)*256 + kv*64 + 64],
                                      rhs=pT_at[:, kjc*128:(kjc+1)*128],
                                      start=(kjc == 0), stop=(kjc == 5),
                                      tile_position=(0, b2), after=tuple(aft))
                        pT_free = t
                        eo_ = ac.op("activation",
                                    out=AOs(h // 2, ab*256+qh*128, ab*256+qh*128+128)[b2:b2+64, :],
                                    in_=ps_all[b2:b2+64, 5, 0:128], func=AF.Identity, after=(t,))
                        psl[5] = eo_
                        at_toks.append(eo_)
                        if qh == 1 and ab == NAB - 1 and hi == 1:
                            qj_free = t
                    it += 1
        A_free_final = list(A_free)
        ao_done = tuple(at_toks[-4:])
        wo_toks = []
        for oc in range(DC):
            nm = "wo%d" % oc
            for qc in range(2):
                t = None
                bk = oc % 2
                for dc in range(DC):
                    aft = [wwait(nm)] + (list(ao_done) if (oc == 0 and qc == 0 and dc == 0) else [])
                    if dc == 0 and psl[bk] is not None:
                        aft.append(psl[bk])
                    t = pe.op("matmul", out=ps_all[:, bk, :], lhsT=wlh(nm, dc, 0, 128),
                              rhs=AOs(dc, qc*512, qc*512+512), start=(dc == 0), stop=(dc == 7),
                              after=tuple(aft))
                e = dv.op("tensor_tensor", out=ACCs(oc, qc*512, qc*512+512),
                          in0=ACCs(oc, qc*512, qc*512+512), in1=ps_all[:, bk, :],
                          op=ALU.add, after=(t,))
                psl[bk] = e
                wo_toks.append(e)
            wfree[nm] = t
        at_done = tuple(wo_toks[-2:])
    else:
        at_done = ssm_done

    conv_done = at_done
    if "conv" in ENABLE:
        cv_toks = []
        for dc in range(DC):
            tmp = es_sb[:, 0:1024]
            tmp2 = es_sb[:, 1024:2048]
            t = dv.op("tensor_scalar", out=tmp, in0=XS(dc, HALO-3, HALO-3+T),
                      scalar1=wconv_c[:, dc*4:dc*4+1], scalar2=None, op0=ALU.mult,
                      after=(x_all[-1],) + ((cv_toks[-1],) if cv_toks else ()))
            for j in range(1, 4):
                t = dv.op("scalar_tensor_tensor", out=tmp, in0=XS(dc, HALO-3+j, HALO-3+j+T),
                          scalar=wconv_c[:, dc*4+j:dc*4+j+1], in1=tmp,
                          op0=ALU.mult, op1=ALU.add, after=(t,))
            ts_ = ac.op("activation", out=tmp2, in_=tmp, func=AF.Silu,
                        bias=COLS["b_conv"][:, dc:dc+1], after=(t,))
            ta = dv.op("scalar_tensor_tensor", out=ACCs(dc, 0, T), in0=tmp2,
                       scalar=0.15, in1=ACCs(dc, 0, T), op0=ALU.mult, op1=ALU.add,
                       after=(ts_,))
            cv_toks.append(ta)
        conv_done = tuple(cv_toks[-1:])

    mem_done = conv_done
    if "mem" in ENABLE:
        wm_toks = []
        for oc in range(DC):
            nm = "wwm%d" % oc
            for qc in range(2):
                t = None
                bk = oc % 2
                for dc in range(DC):
                    aft = [wwait(nm)] + (list(conv_done) if (oc == 0 and qc == 0 and dc == 0) else [])
                    if dc == 0 and psl[bk] is not None:
                        aft.append(psl[bk])
                    t = pe.op("matmul", out=ps_all[:, bk, :], lhsT=wlh(nm, dc, 0, 128),
                              rhs=XCr(dc, qc*512, qc*512+512), start=(dc == 0), stop=(dc == 7),
                              after=tuple(aft))
                e = dv.op("scalar_tensor_tensor", out=ACCs(oc, qc*512, qc*512+512),
                          in0=ps_all[:, bk, :], scalar=COLS["b_wm2"][:, oc:oc+1],
                          in1=ACCs(oc, qc*512, qc*512+512), op0=ALU.add, op1=ALU.add,
                          after=(t,))
                psl[bk] = e
                wm_toks.append(e)
            wfree[nm] = t
        wm_done = tuple(wm_toks[-2:])

        decf2 = decf_s.rearrange("p (s c) -> p s c", s=2)
        decs2 = decs_s.rearrange("p (s c) -> p s c", s=2)
        es_evac_last = None
        eo_last = {}
        es_mm_done = {}
        for qb in range(8):
            es_ev = []
            for dc in range(DC):
                ch = qb * 8 + dc
                csl = ch % 2
                hD = sp.dma("mDecF", decf2[:, csl, :], ("decf", dc),
                            after=((("cur", es_mm_done[ch-2]) if ch >= 2 else ("prevDecF", csl)),))
                for scq in range(8):
                    aft = [("spd", "mDecF", ch + 1)]
                    if dc == 0:
                        if qb == 0 and scq == 0:
                            aft += list(wm_done) + ([si_done[0]] if False else [])
                        if psl[scq] is not None:
                            aft.append(psl[scq])
                        if qb > 0 and scq == 0:
                            aft.append(es_evac_last)
                    tmm = pe.op("matmul", out=ps_all[:, scq, :],
                                lhsT=XCr(dc, qb*128, qb*128+128),
                                rhs=decf2[:, csl, scq*512:(scq+1)*512],
                                start=(dc == 0), stop=(dc == 7), after=tuple(aft))
                es_mm_done[qb*8+dc] = tmm
            for scq in range(8):
                _aft_ev = [es_mm_done[qb*8+7]]
                if scq == 0 and qb > 0:
                    _aft_ev.append(prev_tex_es)
                e = dv.op("tensor_copy", out=es_sb[:, scq*512:(scq+1)*512],
                          in_=ps_all[:, scq, :], after=tuple(_aft_ev))
                psl[scq] = e
                es_ev.append(e)
            es_evac_last = es_ev[-1]
            tmx = dv.op("tensor_reduce", out=col_sc[:, 4:5], in_=es_sb[:], axis=AX.X,
                        op=ALU.max, negate=True, after=(es_ev[-1],))
            tex = ac.op("activation", out=p_es[:], in_=es_sb[:], func=AF.Exp,
                        bias=col_sc[:, 4:5], accum_out=col_sc[:, 5:6], after=(tmx,))
            prev_tex_es = tex
            trc = dv.op("reciprocal", out=col_sc[:, 6:7], in_=col_sc[:, 5:6], after=(tex,))
            tps = dv.op("tensor_scalar", out=p_es[:], in0=p_es[:], scalar1=col_sc[:, 6:7],
                        scalar2=None, op0=ALU.mult, after=(trc,))
            tevs = []
            for g in range(4):
                ttr = None
                for j in range(8):
                    sc_i = g * 8 + j
                    aft = [tps]
                    if g == 0 and j == 0 and qb > 0:
                        aft.append(prev_qb_eo)
                    if j == 0 and psl[4] is not None:
                        aft.append(psl[4])
                    ttr = pe.op("transpose", out=psT[:, j*128:(j+1)*128],
                                in_=p_es[:, sc_i*128:(sc_i+1)*128], identity=ident_b,
                                after=tuple(aft))
                tev = ac.op("activation", out=pT_es[:, g*1024:(g+1)*1024], in_=psT[:],
                            func=AF.Identity, after=(ttr,))
                psl[4] = tev
                tevs.append(tev)
            t_eo = None
            for sc_i in range(32):
                ssl = sc_i % 2
                hS = sp.dma("mDecS", decs2[:, ssl, :], ("decs", sc_i),
                            after=((("cur", eo_last[sc_i-2]) if (sc_i-2) in eo_last else
                                    (("cur", eo_last[30+sc_i]) if (30+sc_i) in eo_last else ("prevDecS", ssl))),))
                for dc in range(DC):
                    aft = [("spd", "mDecS", qb*32 + sc_i + 1), tevs[sc_i // 8]]
                    if sc_i == 0 and psl[dc] is not None:
                        aft.append(psl[dc])
                    t_eo = pe.op("matmul", out=ps_all[:, dc, 0:128],
                                 lhsT=decs2[:, ssl, dc*128:(dc+1)*128],
                                 rhs=pT_es[:, sc_i*128:(sc_i+1)*128],
                                 start=(sc_i == 0), stop=(sc_i == 31), after=tuple(aft))
                eo_last[sc_i] = t_eo
            eo_ev = []
            for dc in range(DC):
                e = dv.op("tensor_tensor", out=ACCs(dc, qb*128, qb*128+128),
                          in0=ACCs(dc, qb*128, qb*128+128), in1=ps_all[:, dc, 0:128],
                          op=ALU.add, after=(t_eo,))
                psl[dc] = e
                eo_ev.append(e)
            mem_done = tuple(eo_ev[-1:])
            prev_qb_eo = t_eo
    ffn_done = mem_done
    if "ffn" in ENABLE:
        l2_sums = []
        for qc in range(2):
            t = None
            for dc in range(DC):
                aft = list(mem_done) if (qc == 0 and dc == 0) else []
                if dc == 0 and psl[4] is not None:
                    aft.append(psl[4])
                t = pe.op("matmul", out=ps_all[0:1, 4, :], lhsT=onesc[:, 0:1],
                          rhs=ACCs(dc, qc*512, qc*512+512), start=(dc == 0), stop=(dc == 7),
                          after=tuple(aft))
            t2 = dv.op("tensor_copy", out=mu_b[0:1, qc*512:qc*512+512], in_=ps_all[0:1, 4, :],
                       after=(t,))
            psl[4] = t2
            l2_sums.append(t2)
        sq_l = None
        for qc in range(2):
            t = None
            for dc in range(DC):
                aft = list(mem_done) if (qc == 0 and dc == 0) else []
                if sq_l is not None:
                    aft.append(sq_l)
                tq = ac.op("activation", out=sq_s[:, 0:512], in_=ACCs(dc, qc*512, qc*512+512),
                           func=AF.Square, after=tuple(aft))
                aft2 = [tq]
                if dc == 0 and psl[5] is not None:
                    aft2.append(psl[5])
                t = pe.op("matmul", out=ps_all[0:1, 5, :], lhsT=onesc[:, 0:1], rhs=sq_s[:, 0:512],
                          start=(dc == 0), stop=(dc == 7), after=tuple(aft2))
                sq_l = t
            t2 = dv.op("tensor_copy", out=r_b[0:1, qc*512:qc*512+512], in_=ps_all[0:1, 5, :],
                       after=(t,))
            psl[5] = t2
            l2_sums.append(t2)
        l2rows = ln_rows2(tuple(l2_sums), T)
        l2bc = []
        for src in (mu_b, r_b):
            for ch in range(2):
                aft = [l2rows]
                if psl[6] is not None:
                    aft.append(psl[6])
                t = pe.op("matmul", out=ps_all[:, 6, :], lhsT=onesr_f[0:1, :],
                          rhs=src[0:1, ch*512:(ch+1)*512], start=True, stop=True, after=tuple(aft))
                t2 = dv.op("tensor_copy", out=src[:, ch*512:(ch+1)*512], in_=ps_all[:, 6, :],
                           after=(t,))
                psl[6] = t2
                l2bc.append(t2)
        l2bc = tuple(l2bc)
        yl_toks = []
        for dc in range(DC):
            for qc in range(2):
                a0, a1 = qc*512, qc*512+512
                tt = dv.op("tensor_tensor", out=sq_s[:, 0:512], in0=ACCs(dc, a0, a1),
                           in1=mu_b[:, a0:a1], op=ALU.subtract,
                           after=l2bc if dc == 0 and qc == 0 else ())
                tt = dv.op("tensor_tensor", out=yln[:, dc*T+a0: dc*T+a1], in0=sq_s[:, 0:512],
                           in1=r_b[:, a0:a1], op=ALU.mult, after=(tt,))
                yl_toks.append(tt)
        yln_done = tuple(yl_toks[-2:])
        rt_toks = []
        for qc in range(2):
            t = None
            for dc in range(DC):
                aft = list(yln_done) if (qc == 0 and dc == 0) else []
                if dc == 0 and psl[7] is not None:
                    aft.append(psl[7])
                t = pe.op("matmul", out=ps_all[0:8, 7, :], lhsT=routw_b[:, dc*8:(dc+1)*8],
                          rhs=yln[:, dc*T+qc*512: dc*T+qc*512+512], start=(dc == 0), stop=(dc == 7),
                          after=tuple(aft))
            t2 = dv.op("tensor_copy", out=rt_row[0:8, qc*512:qc*512+512], in_=ps_all[0:8, 7, :],
                       after=(t,))
            psl[7] = t2
            rt_toks.append(t2)
        gate_toks = []
        for tb in range(8):
            aft = [rt_toks[-1]]
            if psl[6] is not None:
                aft.append(psl[6])
            t = pe.op("transpose", out=ps_all[:, 6, 0:8], in_=rt_row[0:8, tb*128:(tb+1)*128],
                      identity=ident_f[0:8, 0:8], after=tuple(aft))
            t2 = dv.op("tensor_copy", out=rt_tm[:, tb*8:(tb+1)*8], in_=ps_all[:, 6, 0:8], after=(t,))
            psl[6] = t2
            rr = rt_tm[:, tb*8:(tb+1)*8]
            tm1 = dv.op("tensor_reduce", out=col_sc[:, 8:9], in_=rr, axis=AX.X, op=ALU.max,
                        negate=True, after=(t2,))
            tex = ac.op("activation", out=rr, in_=rr, func=AF.Exp, bias=col_sc[:, 8:9],
                        accum_out=col_sc[:, 9:10], after=(tm1,))
            trc = dv.op("reciprocal", out=col_sc[:, 10:11], in_=col_sc[:, 9:10], after=(tex,))
            tpr = dv.op("tensor_scalar", out=rr, in0=rr, scalar1=col_sc[:, 10:11], scalar2=None,
                        op0=ALU.mult, after=(trc,))
            tv = dv.op("tensor_copy", out=rt_sc[:, 0:8], in_=rr, after=(tpr,))
            tk = tv
            for kk in range(3):
                tm = dv.op("tensor_reduce", out=col_sc[:, 12:13], in_=rt_sc[:, 0:8], axis=AX.X,
                           op=ALU.max, after=(tk,))
                tmask = dv.op("tensor_scalar", out=rt_sc[:, 8:16], in0=rt_sc[:, 0:8],
                              scalar1=col_sc[:, 12:13], scalar2=None, op0=ALU.is_ge, after=(tm,))
                tk = dv.op("scalar_tensor_tensor", out=rt_sc[:, 0:8], in0=rt_sc[:, 8:16],
                           scalar=-1e30, in1=rt_sc[:, 0:8], op0=ALU.mult, op1=ALU.add,
                           after=(tmask,))
            tm4 = dv.op("tensor_reduce", out=col_sc[:, 13:14], in_=rt_sc[:, 0:8], axis=AX.X,
                        op=ALU.max, after=(tk,))
            tge = dv.op("tensor_scalar", out=rt_sc[:, 16:24], in0=rr,
                        scalar1=col_sc[:, 13:14], scalar2=None, op0=ALU.is_ge, after=(tm4,))
            tgs = dv.op("tensor_tensor", out=rt_sc[:, 24:32], in0=rr, in1=rt_sc[:, 16:24],
                        op=ALU.mult, after=(tge,))
            tsum = dv.op("tensor_reduce", out=col_sc[:, 14:15], in_=rt_sc[:, 24:32], axis=AX.X,
                         op=ALU.add, after=(tgs,))
            trc2 = dv.op("reciprocal", out=col_sc[:, 15:16], in_=col_sc[:, 14:15], after=(tsum,))
            tgn = dv.op("tensor_scalar", out=gt_tm[:, tb*8:(tb+1)*8] if False else gt_tm.rearrange("p (tb e) -> p tb e", tb=8)[:, tb, :],
                        in0=rt_sc[:, 24:32], scalar1=col_sc[:, 15:16], scalar2=None,
                        op0=ALU.mult, after=(trc2,))
            aft = [tgn]
            if psl[4] is not None:
                aft.append(psl[4])
            tgt = pe.op("transpose", out=psT[0:8, 0:128],
                        in_=gt_tm.rearrange("p (tb e) -> p tb e", tb=8)[:, tb, :],
                        identity=ident_b, after=tuple(aft))
            tge2 = ac.op("activation", out=g_sb[0:8, tb*128:(tb+1)*128], in_=psT[0:8, 0:128],
                         func=AF.Identity, after=(tgt,))
            psl[4] = tge2
            gate_toks.append(tge2)
        gates_done = tuple(gate_toks)

        ffn_sc = bfc(u3[:, 4288:4544])
        dtile = lambda i: (bfc(R_x[:])[:, i*1024:(i+1)*1024] if i < 12
                           else bfc(R_uao[:])[:, (i-12)*1024:(i-12+1)*1024])
        rsel8v = rsel8_b.rearrange("p (s c) -> p s c", s=2)
        prev_up_mm = {}
        prev_dn_mm = {}
        g8_last = {}
        prev_hfm = {}
        ffn_acc_toks = []
        for e in range(NE):
            hg8 = sp.dma("mG8", rsel8v[0:8, e % 2, :],
                         IN["rsel8"].rearrange("n (k c) -> n k c", k=NE)[e, 0:8, :],
                         after=((("cur", g8_last[e % 2]) if (e % 2) in g8_last else ("prevG8", e % 2)),))
            gb_toks = []
            for half5 in range(2):
                aft = [("spd", "mG8", e + 1)]
                if e == 0 and half5 == 0:
                    aft += list(gates_done)
                if half5 == 0 and e > 0 and ffn_acc_toks:
                    aft.append(ffn_acc_toks[-1])
                if psl[6] is not None:
                    aft.append(psl[6])
                t = pe.op("matmul", out=ps_all[:, 6, :], lhsT=rsel8v[0:8, e % 2, :],
                          rhs=g_sb[0:8, half5*512:(half5+1)*512], start=True, stop=True,
                          after=tuple(aft))
                t2 = ac.op("activation", out=g_bc[:, half5*512:(half5+1)*512],
                           in_=ps_all[:, 6, :], func=AF.Identity, after=(t,))
                psl[6] = t2
                gb_toks.append(t2)
            g8_last[e % 2] = t
            gbc_done = tuple(gb_toks)
            for hf in range(2):
                dl1 = sp.dma("mFfn", bfc(R_x[:]).rearrange("p (i c) -> p i c", i=6)[:, :, :] if False else bfc(R_x[:])[:, 0:12288].rearrange("p (i c) -> p i c", i=12),
                             ("down12", e, hf),
                             after=((("cur", prev_dn_mm.get("a")) if prev_dn_mm.get("a") is not None else ("prevDn", 0)),))
                dl2 = sp.dma("mFfn", bfc(R_uao[:])[:, 0:4096].rearrange("p (i c) -> p i c", i=4),
                             ("down4", e, hf),
                             after=((("cur", prev_dn_mm.get("b")) if prev_dn_mm.get("b") is not None else ("prevDn", 1)),))
                up_toks = []
                for hcg in range(8):
                    usl = hcg % 2
                    ul = sp.dma("mFfn", wsl[:, usl*2048:usl*2048+2048].rearrange("p (dc c) -> p dc c", dc=DC),
                                ("up", e, hf, hcg),
                                after=((("cur", prev_up_mm.get(usl)) if prev_up_mm.get(usl) is not None else ("prevUp", usl)),))
                    for hcl in range(2):
                        hcli = hcg * 2 + hcl
                        for qc in range(2):
                            bk = (hcli * 2 + qc) % 4
                            t = None
                            for dc in range(DC):
                                aft = [("spd", "mFfn", (e*2+hf)*10 + 2 + hcg + 1)]
                                if dc == 0:
                                    if psl[bk] is not None:
                                        aft.append(psl[bk])
                                    if e == 0 and hf == 0 and hcg == 0 and hcl == 0 and qc == 0:
                                        aft += list(gbc_done)
                                    ph = prev_hfm.get(hcli * 2 + qc)
                                    if ph is not None:
                                        aft.append(ph)
                                t = pe.op("matmul", out=ps_all[:, bk, :],
                                          lhsT=wsl[:, usl*2048 + dc*256 + hcl*128: usl*2048 + dc*256 + hcl*128 + 128],
                                          rhs=yln[:, dc*T+qc*512: dc*T+qc*512+512],
                                          start=(dc == 0), stop=(dc == 7), after=tuple(aft))
                            tg = ac.op("activation", out=h_fm[:, hcli*1024 + qc*512: hcli*1024 + qc*512 + 512],
                                       in_=ps_all[:, bk, :], func=AF.Gelu_apprx_tanh, after=(t,))
                            psl[bk] = tg
                            up_toks.append(tg)
                            prev_hfm[hcli * 2 + qc] = None
                    prev_up_mm[usl] = t
                up_done = tuple(up_toks)
                dn_last = None
                for oc in range(DC):
                    for qc in range(2):
                        bk = 4 + ((oc * 2 + qc) % 4)
                        t = None
                        for hcli in range(16):
                            aft = []
                            if hcli == 0:
                                aft.append(("spd", "mFfn", (e*2+hf)*10 + 2))
                                aft.append(up_toks[hcli if False else -1])
                                if psl[bk] is not None:
                                    aft.append(psl[bk])
                            t = pe.op("matmul", out=ps_all[:, bk, :],
                                      lhsT=dtile(hcli)[:, oc*128:(oc+1)*128] if False else dtile(hcli).rearrange("p (a c) -> p a c", a=DC)[:, oc, :],
                                      rhs=h_fm[:, hcli*1024 + qc*512: hcli*1024 + qc*512 + 512],
                                      start=(hcli == 0), stop=(hcli == 15), after=tuple(aft))
                        if hf == 0:
                            ev = dv.op("tensor_copy", out=ffn_part[:, oc*T+qc*512: oc*T+qc*512+512],
                                       in_=ps_all[:, bk, :], after=(t,))
                        else:
                            e1 = dv.op("tensor_tensor", out=sq_s[:, 0:512], in0=ps_all[:, bk, :],
                                       in1=ffn_part[:, oc*T+qc*512: oc*T+qc*512+512], op=ALU.add,
                                       after=(t,))
                            e2 = dv.op("tensor_tensor", out=ffn_sc[:, 0:512], in0=sq_s[:, 0:512],
                                       in1=g_bc[:, qc*512:(qc+1)*512], op=ALU.mult, after=(e1,))
                            ev = dv.op("tensor_tensor", out=ACCs(oc, qc*512, qc*512+512),
                                       in0=ACCs(oc, qc*512, qc*512+512), in1=ffn_sc[:, 0:512],
                                       op=ALU.add, after=(e2,))
                            ffn_acc_toks.append(ev)
                        psl[bk] = ev
                        dn_last = t
                        for hcli in range(16):
                            prev_hfm[hcli * 2 + qc] = t if qc == 1 and oc == DC - 1 else prev_hfm.get(hcli * 2 + qc)
                prev_dn_mm["a"] = dn_last
                prev_dn_mm["b"] = dn_last
        ffn_done = tuple(ffn_acc_toks[-2:])

    st_evacs = []
    for rr_ in range(4):
        for tb in range(2):
            g = rr_ * 2 + tb
            ttr = None
            for dc in range(DC):
                bk = 2 + dc // 4
                aft = []
                if dc % 4 == 0:
                    if rr_ == 0 and tb == 0 and dc == 0:
                        aft += list(ffn_done)
                    if psl[bk] is not None:
                        aft.append(psl[bk])
                ttr = pe.op("transpose", out=ps_all[:, bk, (dc % 4)*128:(dc % 4)*128+128],
                            in_=ACCs(dc, g*128, g*128+128), identity=ident_f, after=tuple(aft))
                if dc % 4 == 3:
                    osl = g % 2
                    aft2 = [ttr]
                    if g >= 2:
                        aft2.append(("spdo", "mOut", g - 2 + 1))
                    else:
                        aft2.append(("prevOut", osl))
                    ev = dv.op("tensor_copy",
                               out=ostage[:, osl*1024 + (dc // 4)*512: osl*1024 + (dc // 4)*512 + 512],
                               in_=ps_all[:, bk, :], after=tuple(aft2))
                    psl[bk] = ev
                    st_evacs.append(ev)
            hS = sp.dma("mOut", ("store", g), ("ostage", g % 2),
                        after=(("cur", st_evacs[-1]), ("cur", st_evacs[-2])))
    body_end = {"pe": pe.ops[-1][4], "dv": dv.ops[-1][4], "ac": ac.ops[-1][4]}

    if with_dbg:
        regs = {"acc": acc, "y": y, "delta": delta, "x": R_x[:], "xn": R_xn[:, 0:4096],
                "u": R_uao[:], "es": R_del[:], "u3s": R_u3[:, 2048:5120], "xb": R_x[:], "dti": R_xn[0:64, 4096:4608], "bcr": R_u3[:, 4096:5120]}
        for nm in DBG_REQ:
            src_ap = regs[nm]
            cols = src_ap.shape[1]
            sp.dma("mOut", ("dbg", nm, cols, src_ap.shape[0]), src_ap,
                   after=(("cur", st_evacs[-1]), ("cur", body_end["pe"]), ("cur", body_end["dv"]),
                          ("cur", body_end["ac"])))


    _by_sem = {}
    for op in sp.ops:
        _by_sem.setdefault(op[0], []).append(op)
    _mw = _by_sem.get("mW", [])
    _mw_names = []
    for _nm, _ in WORDER:
        _mw_names += [_nm, _nm] if _nm.startswith("wq") else [_nm]
    assert len(_mw_names) == len(_mw), (len(_mw_names), len(_mw))
    def _mw_ops(prefix_list):
        out = []
        for _nm2, _op2 in zip(_mw_names, _mw):
            base = _nm2.rstrip("0123456789")
            if _nm2 in prefix_list or base in prefix_list:
                out.append(_op2)
        return out
    _ma = _by_sem.get("mA", [])
    _order = []
    _order += _by_sem.get("mStage", [])
    _order += _mw_ops(["xproj", "wdt", "wk", "wv"])
    _order += _by_sem.get("mRsel", [])
    _order += _mw_ops(["wsm"])
    _wq_ops = _mw_ops(["wq"])
    for _p in range(8):
        _order += _wq_ops[2 * _p: 2 * _p + 2]
        _order += _ma[8 * _p: 8 * _p + 8]
    _order += _mw_ops(["wo"])
    _order += _mw_ops(["wwm"])
    _placed = set(id(op) for op in _order)
    _order += [op for op in sp.ops if id(op) not in _placed]
    assert len(_order) == len(sp.ops)
    sp.ops = _order

    wname_by_idx = [nm for nm, _ in WORDER]
    consumed = [nm for nm in wname_by_idx if nm in wfree]
    for semn_i, op_i in enumerate(sp.ops):
        pass
    for nm, cols in WORDER:
        i = WIDX[nm]
        prv = None
        j = i - 2
        while j >= 0:
            if wname_by_idx[j] in wfree:
                prv = ("cur", wfree[wname_by_idx[j]])
                break
            j -= 2
        if prv is None:
            j = len(wname_by_idx) - 2 + (i % 2)
            while j >= 0:
                if wname_by_idx[j] in wfree and (j % 2) == (i % 2):
                    prv = ("prev", wfree[wname_by_idx[j]])
                    break
                j -= 2
        if prv is not None:
            for op in sp.ops:
                if op[0] == "mW" and isinstance(op[3], tuple):
                    pass
    mw_ops = [op for op in sp.ops if op[0] == "mW"]
    mwi = 0
    for nm, cols in WORDER:
        ndma = 2 if nm.startswith("wq") else 1
        prv = None
        j = WIDX[nm] - 2
        while j >= 0:
            if wname_by_idx[j] in wfree:
                prv = ("cur", wfree[wname_by_idx[j]])
                break
            j -= 2
        if prv is None:
            cands = [jj for jj in range(len(wname_by_idx)) if (jj % 2) == (WIDX[nm] % 2)
                     and wname_by_idx[jj] in wfree]
            if cands:
                prv = ("prev", wfree[wname_by_idx[cands[-1]]])
        for q in range(ndma):
            if prv is not None:
                mw_ops[mwi][4].append(prv)
            mwi += 1
    if "ffn" in ENABLE:
        xlast = None
        if "mem" in ENABLE:
            xlast = es_mm_done.get(63)
        elif "attn" in ENABLE:
            xlast = wfree.get("wq7")
        ulast = wfree.get("wo7") if "attn" in ENABLE else None
        for op in sp.ops:
            if op[0] == "mFfn" and isinstance(op[3], tuple) and op[3][0] in ("down12", "down4") \
                    and op[3][1] == 0 and op[3][2] == 0:
                if xlast is not None:
                    op[4].append(("cur", xlast))
                if ulast is not None:
                    op[4].append(("cur", ulast))
                op[4].append(("cur", l2rows))
    if "ffn" in ENABLE:
        stage_guard = prev_dn_mm.get("a")
    elif "ssm" in ENABLE:
        stage_guard = si_toks[-1] if si_toks else None
    else:
        stage_guard = None
    for c in range(6):
        sp.ops[c][4] = [a for a in sp.ops[c][4] if not (isinstance(a, tuple) and a[0] == "prevtile")]
        if stage_guard is not None:
            sp.ops[c][4].append(("prev", stage_guard))

    last_tok = {"cPE": pe.ops[-1][4] if pe.ops else None,
                "cDV": dv.ops[-1][4] if dv.ops else None,
                "cAC": ac.ops[-1][4] if ac.ops else None,
                "cPO": po.ops[-1][4] if po.ops else None}
    for R in (pe, dv, ac, po):
        if R.ops:
            for eng2, tk in last_tok.items():
                if tk is not None and eng2 != R.name:
                    R.ops[0][3].append(("prev", tk))

    prev_map = {}
    if "attn" in ENABLE:
        prev_map[("prevA", 0)] = A_free_final[0]
        prev_map[("prevA", 1)] = A_free_final[1]
    if "ssm" in ENABLE:
        prev_map[("prevRsel", 0)] = rs_toks[-2]
        prev_map[("prevRsel", 1)] = rs_toks[-1]
    if "mem" in ENABLE:
        prev_map[("prevDecF", 0)] = es_mm_done.get(62)
        prev_map[("prevDecF", 1)] = es_mm_done.get(63)
        prev_map[("prevDecS", 0)] = eo_last.get(30)
        prev_map[("prevDecS", 1)] = eo_last.get(31)
    if "ffn" in ENABLE:
        prev_map[("prevG8", 0)] = g8_last.get(0)
        prev_map[("prevG8", 1)] = g8_last.get(1)
        prev_map[("prevUp", 0)] = prev_up_mm.get(0)
        prev_map[("prevUp", 1)] = prev_up_mm.get(1)
        prev_map[("prevDn", 0)] = prev_dn_mm.get("a")
        prev_map[("prevDn", 1)] = prev_dn_mm.get("b")
    prev_map[("prevOut", 0)] = st_evacs[-2]
    prev_map[("prevOut", 1)] = st_evacs[-1]

    referenced = {"cPE": set(), "cDV": set(), "cAC": set(), "cPO": set()}

    def note(tok):
        if isinstance(tok, Tok):
            referenced[tok.eng].add(tok.seq)

    for R in (pe, dv, ac, po):
        for op in R.ops:
            for a in op[3]:
                if isinstance(a, Tok):
                    note(a)
                elif isinstance(a, tuple) and len(a) == 2 and isinstance(a[1], Tok):
                    note(a[1])
                elif isinstance(a, tuple) and len(a) == 2 and isinstance(a[0], str) and a[0].startswith("prev"):
                    t = prev_map.get(a)
                    if t is not None:
                        note(t)
    for op in sp.ops:
        for a in op[4]:
            if isinstance(a, tuple) and len(a) == 2 and isinstance(a[1], Tok):
                note(a[1])
            elif isinstance(a, tuple) and a[0] in ("cur", "prev") and isinstance(a[1], Tok):
                note(a[1])
            elif isinstance(a, tuple) and len(a) == 2 and a[0].startswith("prev"):
                t = prev_map.get(a)
                if t is not None:
                    note(t)
    for c in range(6):
        tokk = None
    ordmap = {}
    DELTA = {}
    for nm, R in (("cPE", pe), ("cDV", dv), ("cAC", ac), ("cPO", po)):
        seqs = sorted(referenced[nm])
        ordmap[nm] = {s: i + 1 for i, s in enumerate(seqs)}
        DELTA[nm] = len(seqs)
    for snm in ("mStage", "mW", "mA", "mRsel", "mDecF", "mDecS", "mFfn", "mG8", "mOut"):
        DELTA[snm] = sp.cnt.get(snm, 0) * 16

    PAD = {k: DELTA[k] for k in DELTA}

    def tokbase(tok, prev=False):
        o = ordmap[tok.eng].get(tok.seq)
        assert o is not None, (tok.eng, tok.seq)
        b = PAD[tok.eng] + o
        if prev:
            b -= DELTA[tok.eng]
        return tok.eng, b

    def collect_waits(own, waits):
        best = {}
        for a in waits:
            if a is None:
                continue
            if isinstance(a, Tok):
                if a.eng == own:
                    continue
                nm, b = tokbase(a)
            elif isinstance(a, tuple) and a[0] in ("spd", "spdo"):
                nm = a[1]; b = PAD[nm] + a[2] * 16
            elif isinstance(a, tuple) and a[0] == "cur" and isinstance(a[1], Tok):
                if a[1].eng == own:
                    continue
                nm, b = tokbase(a[1])
            elif isinstance(a, tuple) and a[0] == "prev" and isinstance(a[1], Tok):
                if a[1].eng == own:
                    continue
                nm, b = tokbase(a[1], prev=True)
            elif isinstance(a, tuple) and len(a) == 2 and isinstance(a[0], str) and a[0].startswith("prev"):
                t = prev_map.get(a)
                if t is None or t.eng == own:
                    continue
                nm, b = tokbase(t, prev=True)
            else:
                raise ValueError(a)
            if nm not in best or b > best[nm]:
                best[nm] = b
        return best

    def wait_sems_of(own, R):
        s = set()
        for op in R.ops:
            s |= set(collect_waits(own, op[3]).keys())
        if own in ("cDV", "cAC", "cPO"):
            s.add(own)
        return sorted(s)

    def replay(eng, own, R, tr, rb, rtmp):
        om = ordmap[own]
        serial = own in SERIAL_ENGINES
        seen = {}
        for idx, (method, args, kw, waits, tok) in enumerate(R.ops):
            for nm, b in collect_waits(own, waits).items():
                if seen.get(nm, -1) >= b:
                    continue
                seen[nm] = b
                if isinstance(tr, int):
                    eng.wait_ge(SEM[nm], b + tr * DELTA[nm])
                else:
                    eng.reg_alu(rtmp, rb[nm], b, op=ALU.add)
                    eng.wait_ge(SEM[nm], rtmp)
            if serial:
                eng.reg_alu(rtmp, rb[own], PAD[own] + idx, op=ALU.add)
                eng.wait_ge(SEM[own], rtmp)
            inst = getattr(eng, method)(*args, **kw)
            if tok.seq in om:
                inst.then_inc(SEM[own], 1)

    def emit_waits(eng, own, waits, tr):
        for nm, b in collect_waits(own, waits).items():
            eng.wait_ge(SEM[nm], b + tr * DELTA[nm])

    if "ssm" in ENABLE:
        last_carry = None
        for op in ac.ops:
            if op[0] == "activation" and op[2].get("func") == AF.Identity:
                pass
        referenced["cAC"].add(ac.ops[-1][4].seq)
        if po.ops:
            referenced["cPO"].add(po.ops[-1][4].seq)
    referenced["cPE"].add(pe.ops[-1][4].seq)
    referenced["cDV"].add(dv.ops[-1][4].seq)
    SERIAL_ENGINES = ("cDV", "cAC", "cPO")
    for nm, R in (("cDV", dv), ("cAC", ac), ("cPO", po)):
        for op in R.ops:
            referenced[nm].add(op[4].seq)
    ordmap = {}
    for nm, R in (("cPE", pe), ("cDV", dv), ("cAC", ac), ("cPO", po)):
        seqs = sorted(referenced[nm])
        ordmap[nm] = {q: i + 1 for i, q in enumerate(seqs)}
        DELTA[nm] = len(seqs)
    for k in DELTA:
        PAD[k] = DELTA[k]

    def resolve_sp(mark, s, t):
        if callable(mark):
            return mark(s, t)
        if not isinstance(mark, tuple):
            return mark
        kind = mark[0]
        if kind == "Aload":
            _, h, ab = mark
            gb = t * NAB + ab
            v = 0 if gb == 0 else (2 if gb == NT * NAB - 1 else 1)
            return SC["A"][v, h].rearrange("q p c -> p q c")
        if kind == "decf":
            return SC["decf"][mark[1]]
        if kind == "decs":
            return SC["decs"][mark[1]]
        if kind == "up":
            _, e, hf, hcg = mark
            return SC["up"][e][:, :, hf*2048 + hcg*256: hf*2048 + (hcg+1)*256].rearrange("dc p c -> p dc c")
        if kind == "down12":
            _, e, hf = mark
            return SC["down"][e][hf*16: hf*16+12].rearrange("i p c -> p i c")
        if kind == "down4":
            _, e, hf = mark
            return SC["down"][e][hf*16+12: hf*16+16].rearrange("i p c -> p i c")
        if kind == "store":
            g = mark[1]
            return None
        if kind == "ostage":
            return ostage[:, mark[1]*1024:(mark[1]+1)*1024]
        raise ValueError(mark)

    def emit_sp_tile(sp_e, s, t, qstate):
        ti = s * NT + t
        for semn, k, out_ap, in_ap, waits, slow in sp.ops:
            emit_waits(sp_e, None, waits, ti)
            if qstate.get("last") is not None:
                pn, pv = qstate["last"]
                sp_e.wait_ge(SEM[pn], pv)
            qstate["last"] = (semn, PAD[semn] + (k + 1) * 16 + ti * DELTA[semn])
            if isinstance(out_ap, tuple) and out_ap[0] == "store":
                g = out_ap[1]
                o = out_d[s, t*T + g*128: t*T + (g+1)*128, :]
                i_ = resolve_sp(in_ap, s, t)
            elif isinstance(out_ap, tuple) and out_ap[0] == "dbg":
                nm, cols = out_ap[1], out_ap[2]
                rows = out_ap[3] if len(out_ap) > 3 else 128
                o = dbg[nm][ti, 0:rows, 0:cols]
                i_ = in_ap
            else:
                o = resolve_sp(out_ap, s, t) if isinstance(out_ap, tuple) or callable(out_ap) else out_ap
                i_ = resolve_sp(in_ap, s, t)
            sp_e.dma_start(out=o, in_=i_).then_inc(SEM[semn], 16)

    pe0, dv0, ac0, po0 = Rec("p0PE"), Rec("p0DV"), Rec("p0AC"), Rec("p0PO")
    sp0 = SPRec()
    gp0 = SPRec()
    for nm in ("mP0g", "p0PE", "p0DV", "p0AC", "p0PO"):
        if nm not in SEM:
            SEM[nm] = st.enter_context(nc.semaphore(nm))

    p0_f32a = R_y[:, 0:1024]
    p0_f32b = R_y[:, 1024:2048]
    p0_bfa = bfc(R_y[:, 2048:2560])
    p0_bfb = bfc(R_y[:, 2560:3072])
    ep_fm = bfc(R_x[:, 0:2048])
    g_bf = bfc(R_del[:])
    wdc1_b = bfc(u3[:, 0:256])
    wdc2_b = bfc(u3[:, 256:2304])
    stK = bfc(u3[:, 2304:2560])
    stS = bfc(u3[:, 2560:2816])
    dec06 = bfc(u3[:, 2880:4928])

    def p0dma_s(out_ap, in_ap, after=()):
        return sp0.dma("mP0", out_ap, in_ap, after=after)

    def p0dma_g(out_ap, in_ap, after=()):
        return gp0.dma("mP0g", out_ap, in_ap, after=after)

    h_identb = p0dma_s(ident_b[:], IN["ident"][:, :])
    h_identf = p0dma_s(ident_f[:], IN["identf"][:, :])
    h_onesc = p0dma_s(onesc[:], IN["onesc"][:, :])
    h_onescb = p0dma_s(onescb[:], IN["onescb"][:, :])
    h_onesr = p0dma_s(onesr_f[0:2, :], IN["onesr"][:, :])
    h_epsc = p0dma_s(epsc, IN["epsc"][:, :])
    VECLD = [("ln_g", IN["ln_g"]), ("ln_b", IN["ln_b"]), ("b_dt", IN["b_dt"]),
             ("D_ssm", IN["D_ssm"]), ("b_conv", IN["b_conv"]), ("b_wm2", IN["b_wm"]),
             ("b_dc2", IN["b_dc2"])]
    h_cols = []
    for nm, src in VECLD:
        h_cols.append(p0dma_g(COLS[nm], src.rearrange("(dc p) -> p dc", p=128), after=()))
    h_wconv = p0dma_g(wconv_c.rearrange("p (dc j) -> p dc j", dc=DC), IN["w_conv"].rearrange("(dc p) j -> p dc j", p=128))
    h_routw = p0dma_g(routw_b.rearrange("p (dc e) -> p dc e", dc=DC), IN["router_w"].rearrange("(dc p) e -> p dc e", p=128))
    h_bdc1 = p0dma_g(bdc1_c, IN["b_dc1"].rearrange("(c p) -> p c", p=128))
    t_bwm = ac0.op("activation", out=COLS["b_wm2"], in_=COLS["b_wm2"], func=AF.Identity,
                   scale=0.2, after=(h_cols[5],))
    h_wk = p0dma_g(SC["wk"], IN["wk"].rearrange("(dc p) c -> dc p c", p=128))
    h_wv = p0dma_g(SC["wv"], IN["wv"].rearrange("(dc p) c -> dc p c", p=128))
    h_xp = p0dma_g(SC["xproj"], IN["w_xproj"].rearrange("(dc p) c -> dc p c", p=128))
    h_wdt = p0dma_g(SC["wdt"], IN["w_dt"][:, :])
    for e in range(NE):
        p0dma_g(SC["up"][e], IN["up_w"][e].rearrange("(dc p) h -> dc p h", p=128))
        p0dma_g(SC["down"][e], IN["down_w"][e].rearrange("(i p) d -> i p d", p=128))
    p0dma_g(SC["A"], IN["A3"][:, :, :, :, :])
    prev_exp = None
    prev_tn = None
    for dc in range(DC):
        sl = dc % 2
        hL = p0dma_s(p0_f32a[:, sl*64:sl*64+64] if False else p0_f32a[:, sl*512:sl*512+64],
                     IN["A_log"][dc*128:(dc+1)*128, :],
                     after=(prev_exp,) if prev_exp is not None else ())
        te = ac0.op("activation", out=sq_s[:, 0:64], in_=p0_f32a[:, sl*512:sl*512+64],
                    func=AF.Exp, after=(hL,) if prev_tn is None else (hL, prev_tn))
        tn = dv0.op("tensor_scalar", out=A_fm[:, dc*64:(dc+1)*64], in0=sq_s[:, 0:64],
                    scalar1=-1.0, scalar2=None, op0=ALU.mult, after=(te,))
        prev_exp = te
        prev_tn = tn
        if dc == DC - 2:
            te_slot0_last = te
        if dc == DC - 1:
            te_slot1_last = te
    prev_store = {}
    for wi, (nm, src, dstsc, scale, cols) in enumerate(
            [("wq", IN["wq"], SC["wq"], 0.125, D), ("wo", IN["wo"], SC["wo"], 0.2, D),
             ("wsm", IN["w_ssm_out"], SC["wsm"], 0.45, D), ("wwm", IN["w_wm"], SC["wwm"], 0.2, D)]):
        for dc in range(DC):
            r = wi * 8 + dc
            sl = r % 2
            fsrc = p0_f32a if sl == 0 else p0_f32b
            bdst = p0_bfa if sl == 0 else p0_bfb
            first_guard = None
            if r == 0:
                first_guard = te_slot1_last
            hld = p0dma_s(fsrc[:, :], src[dc*128:(dc+1)*128, :],
                          after=(prev_store.get(sl), first_guard))
            tc = ac0.op("activation", out=bdst[:, 0:1024], in_=fsrc[:, :], func=AF.Identity,
                        scale=scale, after=(hld,))
            hst = p0dma_g(dstsc[dc], tc_marker := bdst[:, 0:1024], after=(tc,))
            prev_store[sl] = hst
    prev_t = None
    for i in range(32):
        sl = i % 2
        bdst = p0_bfa if sl == 0 else p0_bfb
        hld = p0dma_g(bdst[:, 0:128], IN["episodic"][i*128:(i+1)*128, :],
                      after=(prev_t,) if prev_t is not None else ())
        tt = pe0.op("transpose", out=psT[:, 0:128], in_=bdst[:, 0:128], identity=ident_b,
                    after=(hld, h_identb))
        tv = ac0.op("activation", out=ep_fm[:, i*128:(i+1)*128], in_=psT[:, 0:128],
                    func=AF.Identity, after=(tt,))
        prev_t = tv
    ep_done = prev_t
    h1 = p0dma_s(p0_f32a[:, 0:512], IN["w_dc1"][:, :], after=(prev_store.get(0),))
    tdc1 = dv0.op("tensor_copy", out=wdc1_b[:], in_=p0_f32a[:, 0:512], after=(h1,))
    tdc2s = []
    for q in range(4):
        h2 = p0dma_s(p0_f32b[:, :], IN["w_dc2"][q*128:(q+1)*128, :],
                     after=(prev_store.get(1),) if q == 0 else (tdc2s[-1],))
        t2 = dv0.op("tensor_copy", out=wdc2_b[:, q*1024:(q+1)*1024], in_=p0_f32b[:, :],
                    after=(h2,))
        tdc2s.append(t2)
    psl0 = {i: None for i in range(8)}
    bdc2K = pp[:, 2120:2128]
    bdc2S = pp[:, 2128:2136]
    tbK = ac0.op("activation", out=bdc2K, in_=COLS["b_dc2"], func=AF.Identity,
                 scale=1.0/32.0, after=(h_cols[6],))
    tbS = ac0.op("activation", out=bdc2S, in_=COLS["b_dc2"], func=AF.Identity,
                 scale=0.06, after=(h_cols[6],))
    g_toks = []
    for h2c in range(4):
        for scq in range(8):
            bk = scq % 2
            aft = [tdc1, ep_done, h_bdc1]
            if psl0[bk] is not None:
                aft.append(psl0[bk])
            t = pe0.op("matmul", out=ps_all[:, bk, :], lhsT=wdc1_b[:, h2c*128:(h2c+1)*128],
                       rhs=ep_fm[:, scq*512:(scq+1)*512], start=True, stop=True,
                       after=tuple(aft))
            tg = ac0.op("activation", out=g_bf[:, h2c*4096 + scq*512: h2c*4096 + (scq+1)*512],
                        in_=ps_all[:, bk, :], func=AF.Gelu, bias=bdc1_c[:, h2c:h2c+1],
                        after=(t,))
            psl0[bk] = tg
            g_toks.append(tg)
    g_done = tuple(g_toks[-2:])
    for dcc in range(DC):
        ev2 = None
        for scq in range(8):
            bk = 2 + scq % 2
            t = None
            for h2c in range(4):
                aft = list(g_done) + tdc2s if (dcc == 0 and scq == 0 and h2c == 0) else []
                if h2c == 0 and psl0[bk] is not None:
                    aft.append(psl0[bk])
                t = pe0.op("matmul", out=ps_all[:, bk, :],
                           lhsT=wdc2_b[:, h2c*1024 + dcc*128: h2c*1024 + (dcc+1)*128],
                           rhs=g_bf[:, h2c*4096 + scq*512: h2c*4096 + (scq+1)*512],
                           start=(h2c == 0), stop=(h2c == 3), after=tuple(aft))
            ev1 = ac0.op("activation", out=stK[:], in_=ps_all[:, bk, :], func=AF.Identity,
                         scale=1.0/32.0, bias=bdc2K[:, dcc:dcc+1], after=(t, tbK))
            hK = p0dma_g(SC["decf"][dcc][:, scq*512:(scq+1)*512], stK[:], after=(ev1,))
            ev2 = ac0.op("activation", out=dec06[:, scq*512:(scq+1)*512], in_=ps_all[:, bk, :],
                         func=AF.Identity, scale=0.06, bias=bdc2S[:, dcc:dcc+1],
                         after=(t, tbS, hK))
            psl0[bk] = ev2
        prev_h = None
        for sc_i in range(32):
            aft = [ev2]
            if psl0[4] is not None:
                aft.append(psl0[4])
            tt = pe0.op("transpose", out=psT[:, 0:128], in_=dec06[:, sc_i*128:(sc_i+1)*128],
                        identity=ident_b, after=tuple(aft))
            tv = ac0.op("activation", out=stS[:, 0:128], in_=psT[:, 0:128], func=AF.Identity,
                        after=(tt, prev_h) if prev_h is not None else (tt,))
            psl0[4] = tv
            hT = p0dma_g(SC["decs"][sc_i][:, dcc*128:(dcc+1)*128], stS[:, 0:128], after=(tv,))
            prev_h = hT

    ref0 = {"p0PE": set(), "p0DV": set(), "p0AC": set(), "p0PO": set()}
    for R in (pe0, dv0, ac0, po0):
        for op in R.ops:
            ref0[R.name].add(op[4].seq)
    ord0 = {nm: {q: i + 1 for i, q in enumerate(sorted(s))} for nm, s in ref0.items()}
    TOT0 = {nm: len(s) for nm, s in ref0.items()}
    TOT0["mP0"] = sp0.cnt.get("mP0", 0) * 16
    TOT0["mP0g"] = gp0.cnt.get("mP0g", 0) * 16

    def tokval0(tok):
        if tok.eng in ord0:
            return SEM[tok.eng], ord0[tok.eng][tok.seq]
        raise KeyError(tok.eng)

    def emit_waits0(eng, own, waits):
        best = {}
        for a in waits:
            if a is None:
                continue
            if isinstance(a, Tok):
                if a.eng == own:
                    continue
                s, v = tokval0(a)
            elif isinstance(a, tuple) and a[0] == "spd":
                s, v = SEM[a[1]], a[2] * 16
            else:
                raise ValueError(a)
            if id(s) not in best or v > best[id(s)][1]:
                best[id(s)] = (s, v)
        for s, v in best.values():
            eng.wait_ge(s, v)

    def replay0(eng, own, R):
        serial = own in ("p0DV", "p0AC", "p0PO")
        for idx, (method, args, kw, waits, tok) in enumerate(R.ops):
            emit_waits0(eng, own, waits)
            if serial and idx:
                eng.wait_ge(SEM[own], idx)
            inst = getattr(eng, method)(*args, **kw)
            inst.then_inc(SEM[own], 1)

    def emit_spdma0(eng, rec, semn):
        nprev = 0
        for s_, k, out_ap, in_ap, waits, slow in rec.ops:
            emit_waits0(eng, None, waits)
            if nprev:
                eng.wait_ge(SEM[semn], nprev * 16)
            kw = {"allow_slow_non_contiguous": True} if semn == "mP0g" else {}
            eng.dma_start(out=out_ap, in_=in_ap, **kw).then_inc(SEM[semn], 16)
            nprev += 1

    def barrier0(eng):
        eng.wait_ge(SEM["mP0"], TOT0["mP0"])
        eng.wait_ge(SEM["mP0g"], TOT0["mP0g"])
        for nm in ("p0PE", "p0DV", "p0AC"):
            if TOT0[nm]:
                eng.wait_ge(SEM[nm], TOT0[nm])

    with nc.Block() as block:

        @block.sync
        def _(E):
            if RUN_P0:
                emit_spdma0(E, sp0, "mP0")
                barrier0(E)
            if RUN_TILES:
                for nm in ("mStage", "mW", "mA", "mRsel", "mDecF", "mDecS",
                           "mFfn", "mG8", "mOut"):
                    if DELTA.get(nm):
                        E.sem_inc(SEM[nm], PAD[nm])
                qstate = {"last": None}
                for s_ in range(BCn):
                    for t_ in range(NT):
                        emit_sp_tile(E, s_, t_, qstate)
                E.wait_ge(SEM["mOut"], PAD["mOut"] + NTT * DELTA["mOut"])

        @block.gpsimd
        def _(E):
            if RUN_P0:
                emit_spdma0(E, gp0, "mP0g")
                barrier0(E)
            if not RUN_TILES or not po.ops:
                return
            E.sem_inc(SEM["cPO"], PAD["cPO"])
            wsems = wait_sems_of("cPO", po)
            rtmp = E.alloc_register("po_rtmp")
            rb = {nm: E.alloc_register("po_rb_" + nm) for nm in wsems}
            with E.Fori(0, BCn) as s_:
                with E.Fori(0, NT) as t_:
                    tr = s_ * NT + t_
                    for nm in wsems:
                        E.reg_mov(rb[nm], tr * DELTA[nm])
                    replay(E, "cPO", po, tr, rb, rtmp)

        @block.tensor
        def _(E):
            if RUN_P0:
                replay0(E, "p0PE", pe0)
                barrier0(E)
            if not RUN_TILES:
                return
            E.sem_inc(SEM["cPE"], PAD["cPE"])
            wsems = wait_sems_of("cPE", pe)
            rtmp = E.alloc_register("pe_rtmp")
            rb = {nm: E.alloc_register("pe_rb_" + nm) for nm in wsems}
            with E.Fori(0, BCn) as s_:
                with E.Fori(0, NT) as t_:
                    tr = s_ * NT + t_
                    for nm in wsems:
                        E.reg_mov(rb[nm], tr * DELTA[nm])
                    replay(E, "cPE", pe, tr, rb, rtmp)

        @block.vector
        def _(E):
            if RUN_P0:
                replay0(E, "p0DV", dv0)
                barrier0(E)
            if not RUN_TILES:
                return
            E.sem_inc(SEM["cDV"], PAD["cDV"])
            wsems = wait_sems_of("cDV", dv)
            rtmp = E.alloc_register("dv_rtmp")
            rb = {nm: E.alloc_register("dv_rb_" + nm) for nm in wsems}
            with E.Fori(0, BCn) as s_:
                E.wait_ge(SEM["cAC"], s_ * (NT * DELTA["cAC"]) + PAD["cAC"])
                E.memset(carry, 0.0)
                with E.Fori(0, NT) as t_:
                    tr = s_ * NT + t_
                    for nm in wsems:
                        E.reg_mov(rb[nm], tr * DELTA[nm])
                    replay(E, "cDV", dv, tr, rb, rtmp)

        @block.scalar
        def _(E):
            if RUN_P0:
                replay0(E, "p0AC", ac0)
                barrier0(E)
            if not RUN_TILES:
                return
            E.sem_inc(SEM["cAC"], PAD["cAC"])
            wsems = wait_sems_of("cAC", ac)
            rtmp = E.alloc_register("ac_rtmp")
            rb = {nm: E.alloc_register("ac_rb_" + nm) for nm in wsems}
            with E.Fori(0, BCn) as s_:
                with E.Fori(0, NT) as t_:
                    tr = s_ * NT + t_
                    for nm in wsems:
                        E.reg_mov(rb[nm], tr * DELTA[nm])
                    replay(E, "cAC", ac, tr, rb, rtmp)

    st.close()
    return nc, None
